# version 06
# baseline (speedup 1.0000x reference)
import sys
import numpy as np

try:
    import concourse.bass as bass
except ImportError:
    sys.path.insert(0, "/opt/trn_rl_repo")
    import concourse.bass as bass
import concourse.mybir as mybir
from concourse.bass_utils import run_bass_kernel_spmd

F32 = mybir.dt.float32
BF16 = mybir.dt.bfloat16
AF = mybir.ActivationFunctionType
ALU = mybir.AluOpType
AX = mybir.AxisListType

N_CORES = 8
D = 1024
KT = D // 128
E = 8
DECAY = 0.9
INPUT_SCALE = 0.1
EPS = 1e-5


class _Op:
    __slots__ = ("fn", "eng", "sem", "needs_inc", "inc_amt", "token", "waits")

    def __init__(self, fn, eng, sem, dma):
        self.fn = fn
        self.eng = eng
        self.sem = sem
        self.needs_inc = bool(dma)
        self.inc_amt = 16 if dma else 1
        self.token = None
        self.waits = []


_SELF_SYNC = {"pe": False, "dve": True, "act": True, "qs": True, "qg": True}


class Sched:

    def __init__(self, nc):
        self.nc = nc
        self.ops = {k: [] for k in _SELF_SYNC}
        self.sems = {k: nc.alloc_semaphore("sem_" + k) for k in _SELF_SYNC}
        self.last_writer = {}
        self.readers = {}
        self.fillers = []

    def pump(self, n=1):
        for _ in range(n):
            if not self.fillers:
                return
            self.fillers.pop(0)()

    def drain(self):
        while self.fillers:
            self.fillers.pop(0)()

    def _sem(self, name):
        if name not in self.sems:
            self.sems[name] = self.nc.alloc_semaphore("sem_" + name)
        return self.sems[name]

    def emit(self, eng, fn, reads=(), writes=(), dma=False, dsem=None):
        sem = eng
        if dma:
            sem = "d_" + (dsem if dsem is not None else writes[0])
            self._sem(sem)
        op = _Op(fn, eng, sem, dma)
        deps = []
        for k in reads:
            w = self.last_writer.get(k)
            if w is not None:
                deps.append(w)
        for k in writes:
            w = self.last_writer.get(k)
            if w is not None:
                deps.append(w)
            deps.extend(self.readers.get(k, ()))
        for pop in deps:
            if pop is op:
                continue
            if pop.eng == eng and pop.sem == sem and not _SELF_SYNC[eng]:
                continue
            pop.needs_inc = True
            op.waits.append(pop)
        for k in reads:
            self.readers.setdefault(k, []).append(op)
        for k in writes:
            self.last_writer[k] = op
            self.readers[k] = []
        self.ops[eng].append(op)
        return op

    def finalize_tokens(self):
        counts = {}
        for eng, ops in self.ops.items():
            for op in ops:
                if op.needs_inc:
                    c = counts.get(op.sem, 0) + op.inc_amt
                    counts[op.sem] = c
                    op.token = c
        self.totals = counts

    def replay(self, eng, e):
        watermark = {}
        for op in self.ops[eng]:
            want = {}
            for pop in op.waits:
                assert pop.token is not None
                if pop.token > want.get(pop.sem, -1):
                    want[pop.sem] = pop.token
            for s, v in sorted(want.items()):
                if v > watermark.get(s, 0):
                    e.wait_ge(self.sems[s], v)
                    watermark[s] = v
            instr = op.fn(e)
            if op.needs_inc:
                instr.then_inc(self.sems[op.sem], op.inc_amt)


def build_program(SEQ=4096, CHUNK=512, H=4096):
    TPC = CHUNK // 128
    NCH = SEQ // CHUNK
    HC = H // 128
    NX = 2 * TPC
    NB1 = max(1, (E * HC) // 128)
    B1R = min(128, E * HC)

    nc = bass.Bass("TRN2", target_bir_lowering=False)

    d_x = nc.dram_tensor("x", [SEQ, D], F32, kind="ExternalInput").ap()
    d_win = nc.dram_tensor("Win", [D, 2 * D], F32, kind="ExternalInput").ap()
    d_bin = nc.dram_tensor("bin", [1, 2 * D], F32, kind="ExternalInput").ap()
    d_wout = nc.dram_tensor("Wout", [D, D], F32, kind="ExternalInput").ap()
    d_bout = nc.dram_tensor("bout", [1, D], F32, kind="ExternalInput").ap()
    d_gw = nc.dram_tensor("gw", [D, E], F32, kind="ExternalInput").ap()
    d_gb = nc.dram_tensor("gb", [1, E], F32, kind="ExternalInput").ap()
    d_w1 = nc.dram_tensor("w1", [E, D, H], F32, kind="ExternalInput").ap()
    d_b1 = nc.dram_tensor("b1", [E * HC, 128], F32, kind="ExternalInput").ap()
    d_w2 = nc.dram_tensor("w2", [E, H, D], F32, kind="ExternalInput").ap()
    d_b2 = nc.dram_tensor("b2", [E, D], F32, kind="ExternalInput").ap()
    d_at = nc.dram_tensor("cAT", [128, 128], F32, kind="ExternalInput").ap()
    d_dec = nc.dram_tensor("cdec", [1, 128], F32, kind="ExternalInput").ap()
    d_c128 = nc.dram_tensor("c128", [1, 1], F32, kind="ExternalInput").ap()
    d_id = nc.dram_tensor("cid", [128, 128], F32, kind="ExternalInput").ap()
    d_out = nc.dram_tensor("out", [SEQ, D], F32, kind="ExternalOutput").ap()
    d_s1 = nc.dram_tensor("s1scr", [E, HC, 128, KT * 128], BF16).ap()
    d_s2 = nc.dram_tensor("s2scr", [E, HC, 128, D], BF16).ap()

    A = nc.alloc_sbuf_tensor
    win_sb = A("win_sb", [128, KT * 2 * D], BF16)
    wout_sb = A("wout_sb", [128, KT * D], BF16)
    gw_sb = A("gw_sb", [128, KT * E], BF16)
    binrow = A("binrow", [1, 2 * D], BF16)
    boutrow = A("boutrow", [1, D], BF16)
    gbrow = A("gbrow", [1, E], BF16)
    b2r = A("b2r", [E, D], BF16)
    b1r = A("b1r", [B1R, NB1 * 128], F32)
    b1c = A("b1c", [128, E * HC], F32)
    at_sb = A("at_sb", [128, 128], BF16)
    dec_sb = A("dec_sb", [1, 128], BF16)
    c128_sb = A("c128_sb", [1, 1], BF16)
    id_sb = A("id_sb", [128, 128], F32)
    ones_sb = A("ones_sb", [1, 128], BF16)
    x_sb = A("x_sb", [128, NX * D], F32)
    x1_sb = A("x1_sb", [128, TPC * D], F32)
    f32p = A("f32p", [128, 3 * D], F32)
    trp = A("trp", [128, 2 * D], BF16)
    xn2t_sb = A("xn2t_sb", [128, KT * CHUNK], BF16)
    sg_sb = A("sg_sb", [128, D], F32)
    sh_sb = A("sh_sb", [128, D], F32)
    xg_sb = A("xg_sb", [128, TPC * D], BF16)
    mv_sb = A("mv_sb", [128, 2 * TPC * 2], F32)
    rstd_sb = A("rstd_sb", [128, 2 * TPC], F32)
    negmr_sb = A("negmr_sb", [128, 2 * TPC], F32)
    stats_sb = A("stats_sb", [128, 12], F32)
    g_sb = A("g_sb", [128, TPC * E], F32)
    m_sb = A("m_sb", [128, TPC * E], F32)
    mt_sb = A("mt_sb", [E, TPC * 128], BF16)
    gsh_sb = A("gsh_sb", [128, E], F32)
    ez_sb = A("ez_sb", [128, E], F32)
    ismax_sb = A("ismax_sb", [128, E], F32)
    wm_sb = A("wm_sb", [128, E], F32)
    sel_sb = A("sel_sb", [128, E], F32)
    s1_sb = A("s1_sb", [128, 1], F32)
    s2_sb = A("s2_sb", [128, 1], F32)
    s3_sb = A("s3_sb", [128, 1], F32)
    carry_sb = A("carry_sb", [1, 2 * D], BF16)
    ht_sb = A("ht_sb", [128, HC * CHUNK], BF16)
    W1S, W2S = 6, 4
    w1buf = A("w1buf", [128, W1S * KT * 128], BF16)
    w2buf = A("w2buf", [128, W2S * D], BF16)
    eps_c = A("eps_c", [128, 1], F32)

    P1 = nc.alloc_psum_tensor("P1", [128, 2048], F32)
    P2 = nc.alloc_psum_tensor("P2", [128, 2048], F32)

    S = Sched(nc)

    def mm(out_ap, lhsT, rhs, start, stop, reads, writes):
        S.emit("pe",
               lambda e, o=out_ap, l=lhsT, r=rhs, s=start, t=stop:
               e.matmul(o, lhsT=l, rhs=r, start=s, stop=t),
               reads=reads, writes=writes)

    def tr(out_ap, in_ap, reads, writes, nk=128):
        S.emit("pe",
               lambda e, o=out_ap, i=in_ap, nk=nk:
               e.transpose(o, i, id_sb[0:nk, 0:nk]),
               reads=list(reads) + ["id"], writes=writes)

    P1K = ["P1a", "P1b", "P1c", "P1d"]
    P2K = ["P2a", "P2b", "P2c", "P2d"]

    def cast_load(dst_ap, src_ap, wkey):
        S.emit("qg", lambda e, o=dst_ap, i=src_ap: e.dma_start(out=o, in_=i),
               reads=(), writes=[wkey], dma=True)

    cast_load(win_sb[:].rearrange("p (k e) -> p k e", k=KT),
              d_win.rearrange("(k p) e -> p k e", p=128), "win")
    cast_load(wout_sb[:].rearrange("p (k e) -> p k e", k=KT),
              d_wout.rearrange("(k p) e -> p k e", p=128), "wout")
    cast_load(gw_sb[:].rearrange("p (k e) -> p k e", k=KT),
              d_gw.rearrange("(k p) e -> p k e", p=128), "gw")
    cast_load(binrow[:], d_bin[:], "binrow")
    cast_load(boutrow[:], d_bout[:], "boutrow")
    cast_load(gbrow[:], d_gb[:], "gbrow")
    cast_load(b2r[:], d_b2[:], "b2r")
    cast_load(at_sb[:], d_at[:], "at")
    cast_load(dec_sb[:], d_dec[:], "dec")
    cast_load(c128_sb[:], d_c128[:], "c128")
    S.emit("qs", lambda e: e.dma_start(out=id_sb[:], in_=d_id[:]),
           reads=(), writes=["id"], dma=True)
    for i in range(NB1):
        S.emit("qs",
               lambda e, i=i: e.dma_start(
                   out=b1r[:, i * 128:(i + 1) * 128],
                   in_=d_b1[i * 128:i * 128 + B1R, :]),
               reads=(), writes=["b1r%d" % i], dma=True)
    S.emit("dve", lambda e: e.memset(ones_sb[:], 1.0), writes=["ones"])
    S.emit("dve", lambda e: e.memset(eps_c[:], EPS), writes=["eps"])
    for i in range(NB1):
        tr(P2[0:128, 0:B1R], b1r[0:B1R, i * 128:(i + 1) * 128],
           reads=["b1r%d" % i], writes=["P2a"], nk=B1R)
        S.emit("dve",
               lambda e, i=i: e.tensor_copy(
                   out=b1c[:, i * 128:i * 128 + B1R], in_=P2[0:128, 0:B1R]),
               reads=["P2a"], writes=["b1c"])

    def load_x(c):
        for tt in range(TPC):
            g = c * TPC + tt
            slot = g % NX
            S.emit("qs",
                   lambda e, g=g, slot=slot: e.dma_start(
                       out=x_sb[:, slot * D:(slot + 1) * D],
                       in_=d_x[g * 128:(g + 1) * 128, :]),
                   reads=(), writes=["x%d" % slot], dma=True)

    load_x(0)

    fcnt = [0]
    tcnt = [0]

    def f32slot():
        s = fcnt[0] % 3
        fcnt[0] += 1
        return f32p[:, s * D:(s + 1) * D], "f32p%d" % s

    def trslot():
        s = tcnt[0] % 2
        tcnt[0] += 1
        return trp[:, s * D:(s + 1) * D], "trp%d" % s

    def emit_B(c, part, defer=False):
        def em(eng, fn, reads=(), writes=()):
            if defer:
                S.fillers.append(lambda eng=eng, fn=fn, r=tuple(reads),
                                 w=tuple(writes): S.emit(eng, fn, r, w))
            else:
                S.emit(eng, fn, reads, writes)
        tts = {0: (0, 1), 1: (2, 3)}.get(part, ())
        for tt in tts:
            g = c * TPC + tt
            xk = "x%d" % (g % NX)
            xap = x_sb[:, (g % NX) * D:((g % NX) + 1) * D]
            em("dve", lambda e, x=xap: e.bn_stats(out=stats_sb[:, 0:6], in_=x[:, 0:512]),
               reads=[xk], writes=["stats"])
            em("dve", lambda e, x=xap: e.bn_stats(out=stats_sb[:, 6:12], in_=x[:, 512:1024]),
               reads=[xk], writes=["stats"])
            em("dve", lambda e, tt=tt: e.bn_aggr(
                out=mv_sb[:, 2 * tt:2 * tt + 2], in_=stats_sb[:]),
                reads=["stats"], writes=["mv%d" % tt])
        if part == 2:
            for tt in range(TPC):
                em("act", lambda e, tt=tt: e.activation(
                    out=rstd_sb[:, tt:tt + 1], in_=mv_sb[:, 2 * tt + 1:2 * tt + 2],
                    func=AF.Sqrt, bias=eps_c[:]),
                    reads=["mv%d" % tt, "eps"], writes=["rstd%d" % tt])
            for tt in range(TPC):
                em("dve", lambda e, tt=tt: e.reciprocal(
                    out=rstd_sb[:, tt:tt + 1], in_=rstd_sb[:, tt:tt + 1]),
                    reads=["rstd%d" % tt], writes=["rstd%d" % tt])
                em("dve", lambda e, tt=tt: e.scalar_tensor_tensor(
                    out=negmr_sb[:, tt:tt + 1], in0=mv_sb[:, 2 * tt:2 * tt + 1],
                    scalar=-1.0, in1=rstd_sb[:, tt:tt + 1],
                    op0=ALU.mult, op1=ALU.mult),
                    reads=["mv%d" % tt, "rstd%d" % tt], writes=["negmr%d" % tt])

    def emit_C(c):
        for tt in range(TPC):
            g = c * TPC + tt
            xk = "x%d" % (g % NX)
            xap = x_sb[:, (g % NX) * D:((g % NX) + 1) * D]
            xn_ap, xn_k = f32slot()
            S.emit("act", lambda e, x=xap, o=xn_ap, tt=tt: e.activation(
                out=o, in_=x, func=AF.Identity,
                bias=negmr_sb[:, tt:tt + 1], scale=rstd_sb[:, tt:tt + 1]),
                reads=[xk, "negmr%d" % tt, "rstd%d" % tt], writes=[xn_k])
            for k in range(KT):
                tr(P2[:, 1024 + k * 128:1024 + (k + 1) * 128],
                   xn_ap[:, k * 128:(k + 1) * 128],
                   reads=[xn_k], writes=["P2c" if k < 4 else "P2d"])
            xnt_ap, xnt_k = trslot()
            S.emit("dve", lambda e, o=xnt_ap: e.tensor_copy(
                out=o, in_=P2[:, 1024:2048]),
                reads=["P2c", "P2d"], writes=[xnt_k])
            for nch in range(4):
                mm(P1[:, nch * 512:(nch + 1) * 512], ones_sb[:],
                   binrow[:, nch * 512:(nch + 1) * 512], True, False,
                   reads=["ones", "binrow"], writes=[P1K[nch]])
            for k in range(KT):
                for nch in range(4):
                    mm(P1[:, nch * 512:(nch + 1) * 512],
                       xnt_ap[:, k * 128:(k + 1) * 128],
                       win_sb[:, k * 2 * D + nch * 512:k * 2 * D + (nch + 1) * 512],
                       False, k == KT - 1,
                       reads=[xnt_k, "win"], writes=[P1K[nch]])
            S.emit("act", lambda e: e.activation(
                out=sg_sb[:], in_=P1[:, 0:1024], func=AF.Sigmoid),
                reads=["P1a", "P1b"], writes=["sg"])
            S.emit("act", lambda e: e.activation(
                out=sh_sb[:], in_=P1[:, 1024:2048], func=AF.Sigmoid),
                reads=["P1c", "P1d"], writes=["sh"])
            S.emit("dve", lambda e: e.tensor_tensor(
                out=sh_sb[:], in0=sh_sb[:], in1=P1[:, 1024:2048], op=ALU.mult),
                reads=["sh", "P1c", "P1d"], writes=["sh"])
            xg_ap = xg_sb[:, tt * D:(tt + 1) * D]
            S.emit("dve", lambda e, o=xg_ap: e.tensor_tensor(
                out=o, in0=sg_sb[:], in1=sh_sb[:], op=ALU.mult),
                reads=["sg", "sh"], writes=["xg%d" % tt])

    def emit_DE_F1(c):
        for tt in range(TPC):
            g = c * TPC + tt
            xg_ap = xg_sb[:, tt * D:(tt + 1) * D]
            xg_k = "xg%d" % tt
            cprev = carry_sb[0:1, ((g + 1) % 2) * D:((g + 1) % 2 + 1) * D]
            cprev_k = "cr%d" % ((g + 1) % 2)
            ccur = carry_sb[0:1, (g % 2) * D:(g % 2 + 1) * D]
            ccur_k = "cr%d" % (g % 2)
            for nch in range(2):
                mm(P2[:, nch * 512:(nch + 1) * 512], at_sb[:],
                   xg_ap[:, nch * 512:(nch + 1) * 512], True, g == 0,
                   reads=["at", xg_k], writes=[P2K[nch]])
                if g > 0:
                    mm(P2[:, nch * 512:(nch + 1) * 512], dec_sb[:],
                       cprev[:, nch * 512:(nch + 1) * 512], False, True,
                       reads=["dec", cprev_k], writes=[P2K[nch]])
            for nch in range(2):
                mm(P2[0:1, 1024 + nch * 512:1024 + (nch + 1) * 512],
                   at_sb[:, 127:128], xg_ap[:, nch * 512:(nch + 1) * 512],
                   True, g == 0,
                   reads=["at", xg_k], writes=[P2K[2 + nch]])
                if g > 0:
                    mm(P2[0:1, 1024 + nch * 512:1024 + (nch + 1) * 512],
                       c128_sb[:], cprev[:, nch * 512:(nch + 1) * 512],
                       False, True,
                       reads=["c128", cprev_k], writes=[P2K[2 + nch]])
            S.emit("act", lambda e, o=ccur: e.activation(
                out=o, in_=P2[0:1, 1024:2048], func=AF.Copy),
                reads=["P2c", "P2d"], writes=[ccur_k])
            hid_ap, hid_k = f32slot()
            S.emit("dve", lambda e, o=hid_ap: e.tensor_copy(
                out=o, in_=P2[:, 0:1024]),
                reads=["P2a", "P2b"], writes=[hid_k])
            for k in range(KT):
                tr(P1[:, 1024 + k * 128:1024 + (k + 1) * 128],
                   hid_ap[:, k * 128:(k + 1) * 128],
                   reads=[hid_k], writes=["P1c" if k < 4 else "P1d"])
            ht_ap, ht_k = trslot()
            S.emit("dve", lambda e, o=ht_ap: e.tensor_copy(
                out=o, in_=P1[:, 1024:2048]),
                reads=["P1c", "P1d"], writes=[ht_k])
            for nch in range(2):
                mm(P1[:, nch * 512:(nch + 1) * 512], ones_sb[:],
                   boutrow[:, nch * 512:(nch + 1) * 512], True, False,
                   reads=["ones", "boutrow"], writes=[P1K[nch]])
            for k in range(KT):
                for nch in range(2):
                    mm(P1[:, nch * 512:(nch + 1) * 512],
                       ht_ap[:, k * 128:(k + 1) * 128],
                       wout_sb[:, k * D + nch * 512:k * D + (nch + 1) * 512],
                       False, k == KT - 1,
                       reads=[ht_k, "wout"], writes=[P1K[nch]])
            xap = x_sb[:, (g % NX) * D:((g % NX) + 1) * D]
            x1_ap = x1_sb[:, tt * D:(tt + 1) * D]
            S.emit("dve", lambda e, x=xap, o=x1_ap: e.tensor_tensor(
                out=o, in0=P1[:, 0:1024], in1=x, op=ALU.add),
                reads=["P1a", "P1b", "x%d" % (g % NX)], writes=["x1_%d" % tt])
            S.emit("dve", lambda e, tt=tt: e.bn_stats(
                out=stats_sb[:, 0:6], in_=x1_sb[:, tt * D:tt * D + 512]),
                reads=["x1_%d" % tt], writes=["stats"])
            S.emit("dve", lambda e, tt=tt: e.bn_stats(
                out=stats_sb[:, 6:12], in_=x1_sb[:, tt * D + 512:tt * D + 1024]),
                reads=["x1_%d" % tt], writes=["stats"])
            S.emit("dve", lambda e, tt=tt: e.bn_aggr(
                out=mv_sb[:, 2 * TPC + 2 * tt:2 * TPC + 2 * tt + 2],
                in_=stats_sb[:]),
                reads=["stats"], writes=["mv2_%d" % tt])
            S.emit("act", lambda e, tt=tt: e.activation(
                out=rstd_sb[:, TPC + tt:TPC + tt + 1],
                in_=mv_sb[:, 2 * TPC + 2 * tt + 1:2 * TPC + 2 * tt + 2],
                func=AF.Sqrt, bias=eps_c[:]),
                reads=["mv2_%d" % tt, "eps"], writes=["rstd2_%d" % tt])
            S.emit("dve", lambda e, tt=tt: e.reciprocal(
                out=rstd_sb[:, TPC + tt:TPC + tt + 1],
                in_=rstd_sb[:, TPC + tt:TPC + tt + 1]),
                reads=["rstd2_%d" % tt], writes=["rstd2_%d" % tt])
            S.emit("dve", lambda e, tt=tt: e.scalar_tensor_tensor(
                out=negmr_sb[:, TPC + tt:TPC + tt + 1],
                in0=mv_sb[:, 2 * TPC + 2 * tt:2 * TPC + 2 * tt + 1],
                scalar=-1.0, in1=rstd_sb[:, TPC + tt:TPC + tt + 1],
                op0=ALU.mult, op1=ALU.mult),
                reads=["mv2_%d" % tt, "rstd2_%d" % tt], writes=["negmr2_%d" % tt])
            xn2_ap, xn2_k = f32slot()
            S.emit("act", lambda e, o=xn2_ap, tt=tt: e.activation(
                out=o, in_=x1_sb[:, tt * D:(tt + 1) * D], func=AF.Identity,
                bias=negmr_sb[:, TPC + tt:TPC + tt + 1],
                scale=rstd_sb[:, TPC + tt:TPC + tt + 1]),
                reads=["x1_%d" % tt, "negmr2_%d" % tt, "rstd2_%d" % tt],
                writes=[xn2_k])
            for k in range(KT):
                tr(P2[:, 1024 + k * 128:1024 + (k + 1) * 128],
                   xn2_ap[:, k * 128:(k + 1) * 128],
                   reads=[xn2_k], writes=["P2c" if k < 4 else "P2d"])
            xn2t_dst = (xn2t_sb[:].rearrange("p (k t) -> p k t", t=CHUNK)
                        [:, :, tt * 128:(tt + 1) * 128])
            S.emit("dve", lambda e, o=xn2t_dst: e.tensor_copy(
                out=o,
                in_=P2[:, 1024:2048].rearrange("p (k t) -> p k t", t=128)),
                reads=["P2c", "P2d"], writes=["xn2t"])

    def emit_F2a(c):
        for tt in range(TPC):
            mm(P2[0:128, 0:E], ones_sb[:], gbrow[:], True, False,
               reads=["ones", "gbrow"], writes=["P2a"])
            for k in range(KT):
                mm(P2[0:128, 0:E],
                   xn2t_sb[:, k * CHUNK + tt * 128:k * CHUNK + (tt + 1) * 128],
                   gw_sb[:, k * E:(k + 1) * E], False, k == KT - 1,
                   reads=["xn2t", "gw"], writes=["P2a"])
            S.emit("dve", lambda e, tt=tt: e.tensor_copy(
                out=g_sb[:, tt * E:(tt + 1) * E], in_=P2[0:128, 0:E]),
                reads=["P2a"], writes=["g%d" % tt])
        def fe(eng, fn, reads=(), writes=()):
            S.fillers.append(lambda eng=eng, fn=fn, r=tuple(reads),
                             w=tuple(writes): S.emit(eng, fn, r, w))
        for tt in range(TPC):
            gk = "g%d" % tt
            gap = g_sb[:, tt * E:(tt + 1) * E]
            fe("dve", lambda e, g=gap: e.reduce_max(
                out=s1_sb[:], in_=g, axis=AX.X), reads=[gk], writes=["s1"])
            fe("dve", lambda e, g=gap: e.tensor_scalar(
                out=gsh_sb[:], in0=g, scalar1=s1_sb[:], scalar2=None,
                op0=ALU.subtract), reads=[gk, "s1"], writes=["gsh"])
            fe("act", lambda e: e.activation(
                out=ez_sb[:], in_=gsh_sb[:], func=AF.Exp),
                reads=["gsh"], writes=["ez"])
            fe("dve", lambda e: e.reduce_sum(
                out=s2_sb[:], in_=ez_sb[:], axis=AX.X), reads=["ez"], writes=["s2"])
            fe("dve", lambda e: e.reciprocal(
                out=s2_sb[:], in_=s2_sb[:]), reads=["s2"], writes=["s2"])
            wap = m_sb[:, tt * E:(tt + 1) * E]
            wk = "m%d" % tt
            fe("dve", lambda e, o=wap: e.tensor_scalar(
                out=o, in0=ez_sb[:], scalar1=s2_sb[:], scalar2=None,
                op0=ALU.mult), reads=["ez", "s2"], writes=[wk])
            fe("dve", lambda e, w=wap: e.reduce_max(
                out=s1_sb[:], in_=w, axis=AX.X), reads=[wk], writes=["s1"])
            fe("dve", lambda e, w=wap: e.tensor_scalar(
                out=ismax_sb[:], in0=w, scalar1=s1_sb[:], scalar2=None,
                op0=ALU.is_ge), reads=[wk, "s1"], writes=["ismax"])
            fe("dve", lambda e, w=wap: e.scalar_tensor_tensor(
                out=wm_sb[:], in0=ismax_sb[:], scalar=-1e9, in1=w,
                op0=ALU.mult, op1=ALU.add),
                reads=["ismax", wk], writes=["wm"])
            fe("dve", lambda e: e.reduce_max(
                out=s3_sb[:], in_=wm_sb[:], axis=AX.X), reads=["wm"], writes=["s3"])
            fe("dve", lambda e, w=wap: e.tensor_scalar(
                out=sel_sb[:], in0=w, scalar1=s3_sb[:], scalar2=None,
                op0=ALU.is_ge), reads=[wk, "s3"], writes=["sel"])
            fe("dve", lambda e, w=wap, o=wap: e.tensor_tensor(
                out=o, in0=w, in1=sel_sb[:], op=ALU.mult),
                reads=[wk, "sel"], writes=[wk])

    def emit_F2b(c):
        for tt in range(TPC):
            wk = "m%d" % tt
            wap = m_sb[:, tt * E:(tt + 1) * E]
            tr(P2[0:E, 0:128], wap, reads=[wk], writes=["P2a"])
            S.emit("dve", lambda e, tt=tt: e.tensor_copy(
                out=mt_sb[0:E, tt * 128:(tt + 1) * 128], in_=P2[0:E, 0:128]),
                reads=["P2a"], writes=["mt%d" % tt])
        for tt in range(TPC):
            for nch in range(2):
                mm(P2[:, nch * 512:(nch + 1) * 512],
                   mt_sb[0:E, tt * 128:(tt + 1) * 128],
                   b2r[0:E, nch * 512:(nch + 1) * 512], True, True,
                   reads=["mt%d" % tt, "b2r"], writes=[P2K[nch]])
            S.emit("dve", lambda e, tt=tt: e.tensor_tensor(
                out=x1_sb[:, tt * D:(tt + 1) * D],
                in0=P2[:, 0:1024], in1=x1_sb[:, tt * D:(tt + 1) * D],
                op=ALU.add),
                reads=["P2a", "P2b", "x1_%d" % tt], writes=["x1_%d" % tt])

    def emit_moe(c):
        w2slots = [("P1", 0), ("P1", 1024), ("P2", 0), ("P2", 1024)]
        for ex in range(E):
            if c + 1 < NCH and ex in (1, 3, 5):
                emit_B(c + 1, {1: 0, 3: 1, 5: 2}[ex], defer=True)
            for hc in range(HC):
                wi = (ex * HC + hc) % W1S
                wk = "w1b%d" % wi
                sk = "s1_%d_%d" % (ex, hc)
                if c == 0:
                    S.emit("qg",
                           lambda e, ex=ex, hc=hc, wi=wi: e.dma_start(
                               out=w1buf[:, wi * KT * 128:(wi + 1) * KT * 128]
                               .rearrange("p (k q) -> p k q", q=128),
                               in_=d_w1[ex, :, hc * 128:(hc + 1) * 128]
                               .rearrange("(k p) q -> p k q", p=128)),
                           reads=(), writes=[wk], dma=True)
                    S.emit("qs",
                           lambda e, ex=ex, hc=hc, wi=wi: e.dma_start(
                               out=d_s1[ex, hc, :, :],
                               in_=w1buf[:, wi * KT * 128:(wi + 1) * KT * 128]),
                           reads=[wk], writes=[sk], dma=True,
                           dsem="sw1_%d" % wi)
                else:
                    S.emit("qg",
                           lambda e, ex=ex, hc=hc, wi=wi: e.dma_start(
                               out=w1buf[:, wi * KT * 128:(wi + 1) * KT * 128],
                               in_=d_s1[ex, hc, :, :]),
                           reads=[sk], writes=[wk], dma=True)
                ps = P1K[hc % 4]
                pof = 512 * (hc % 4)
                for k in range(KT):
                    mm(P1[:, pof:pof + CHUNK],
                       w1buf[:, wi * KT * 128 + k * 128:wi * KT * 128 + (k + 1) * 128],
                       xn2t_sb[:, k * CHUNK:(k + 1) * CHUNK],
                       k == 0, k == KT - 1,
                       reads=[wk, "xn2t"], writes=[ps])
                S.pump(2)
                hk = "ht%d" % hc
                if hc % 2 == 0:
                    S.emit("act", lambda e, ex=ex, hc=hc, pof=pof: e.activation(
                        out=ht_sb[:, hc * CHUNK:(hc + 1) * CHUNK],
                        in_=P1[:, pof:pof + CHUNK], func=AF.Relu,
                        bias=b1c[:, ex * HC + hc:ex * HC + hc + 1]),
                        reads=[ps, "b1c"], writes=[hk])
                else:
                    S.emit("dve", lambda e, ex=ex, hc=hc, pof=pof: e.tensor_scalar(
                        out=ht_sb[:, hc * CHUNK:(hc + 1) * CHUNK],
                        in0=P1[:, pof:pof + CHUNK],
                        scalar1=b1c[:, ex * HC + hc:ex * HC + hc + 1],
                        scalar2=0.0, op0=ALU.add, op1=ALU.max),
                        reads=[ps, "b1c"], writes=[hk])
            if ex == 0:
                S.drain()
                emit_F2b(c)
            for kt in range(HC):
                wi = kt % W2S
                wk = "w2b%d" % wi
                sk = "s2_%d_%d" % (ex, kt)
                if c == 0:
                    S.emit("qg",
                           lambda e, ex=ex, kt=kt, wi=wi: e.dma_start(
                               out=w2buf[:, wi * D:(wi + 1) * D],
                               in_=d_w2[ex, kt * 128:(kt + 1) * 128, :]),
                           reads=(), writes=[wk], dma=True)
                    S.emit("qs",
                           lambda e, ex=ex, kt=kt, wi=wi: e.dma_start(
                               out=d_s2[ex, kt, :, :],
                               in_=w2buf[:, wi * D:(wi + 1) * D]),
                           reads=[wk], writes=[sk], dma=True,
                           dsem="sw2_%d" % wi)
                else:
                    S.emit("qg",
                           lambda e, ex=ex, kt=kt, wi=wi: e.dma_start(
                               out=w2buf[:, wi * D:(wi + 1) * D],
                               in_=d_s2[ex, kt, :, :]),
                           reads=[sk], writes=[wk], dma=True)
                for tt in range(TPC):
                    pn, po = w2slots[tt]
                    P = P1 if pn == "P1" else P2
                    pk = (P1K if pn == "P1" else P2K)
                    for nch in range(2):
                        mm(P[:, po + nch * 512:po + (nch + 1) * 512],
                           ht_sb[:, kt * CHUNK + tt * 128:kt * CHUNK + (tt + 1) * 128],
                           w2buf[:, wi * D + nch * 512:wi * D + (nch + 1) * 512],
                           kt == 0, kt == HC - 1,
                           reads=["ht%d" % kt, wk],
                           writes=[pk[po // 512 + nch]])
            tto = list(range(TPC)) if ex < E - 1 else list(range(TPC - 1, -1, -1))
            for tt in tto:
                pn, po = w2slots[tt]
                P = P1 if pn == "P1" else P2
                pk = (P1K if pn == "P1" else P2K)
                tmp_ap, tmp_k = f32slot()
                S.emit("act", lambda e, P=P, po=po, o=tmp_ap: e.copy(
                    out=o, in_=P[:, po:po + 1024]),
                    reads=[pk[po // 512], pk[po // 512 + 1]],
                    writes=[tmp_k])
                mcol = m_sb[:, tt * E + ex:tt * E + ex + 1]
                S.emit("dve", lambda e, m=mcol, tt=tt, i=tmp_ap:
                       e.scalar_tensor_tensor(
                           out=x1_sb[:, tt * D:(tt + 1) * D],
                           in0=i, scalar=m,
                           in1=x1_sb[:, tt * D:(tt + 1) * D],
                           op0=ALU.mult, op1=ALU.add),
                       reads=[tmp_k, "m%d" % tt, "x1_%d" % tt],
                       writes=["x1_%d" % tt])

    for c in range(NCH):
        if c + 1 < NCH:
            load_x(c + 1)
        if c == 0:
            for part in range(3):
                emit_B(0, part)
        emit_C(c)
        emit_DE_F1(c)
        emit_F2a(c)
        emit_moe(c)
        S.drain()
        for tt in range(TPC):
            g = c * TPC + tt
            S.emit("qs",
                   lambda e, g=g, tt=tt: e.dma_start(
                       out=d_out[g * 128:(g + 1) * 128, :],
                       in_=x1_sb[:, tt * D:(tt + 1) * D]),
                   reads=["x1_%d" % tt], writes=["outd%d" % g], dma=True,
                   dsem="o%d" % tt)

    S.finalize_tokens()
    out_totals = [(S.sems["d_o%d" % tt], S.totals["d_o%d" % tt])
                  for tt in range(TPC)]

    def _final(e, pairs=out_totals):
        for sem, v in pairs:
            e.wait_ge(sem, v)
        return e.nop()

    fin = _Op(_final, "qs", "qs", False)
    S.ops["qs"].append(fin)

    with nc.Block() as blk:
        @blk.tensor
        def _(e):
            S.replay("pe", e)

        @blk.vector
        def _(e):
            S.replay("dve", e)

        @blk.scalar
        def _(e):
            S.replay("act", e)

        @blk.sync
        def _(e):
            S.replay("qs", e)

        @blk.gpsimd
        def _(e):
            S.replay("qg", e)

    return nc


def make_consts():
    t = np.arange(128)
    i = np.arange(128)
    AT = np.where(t[None, :] >= i[:, None],
                  INPUT_SCALE * DECAY ** (t[None, :] - i[:, None]), 0.0
                  ).astype(np.float32)
    dec = (DECAY ** (t + 1)).astype(np.float32)[None, :]
    c128 = np.array([[DECAY ** 128]], dtype=np.float32)
    ident = np.eye(128, dtype=np.float32)
    return AT, dec, c128, ident


def make_in_maps(x, Win, b_in, Wout, b_out, gw, gb, w1, b1, w2, b2,
                 SEQ=4096, H=4096):
    HC = H // 128
    AT, dec, c128, ident = make_consts()
    f32 = lambda a: np.ascontiguousarray(np.asarray(a, dtype=np.float32))
    common = dict(
        Win=f32(Win), bin=f32(b_in).reshape(1, 2 * D),
        Wout=f32(Wout), bout=f32(b_out).reshape(1, D),
        gw=f32(gw), gb=f32(gb).reshape(1, E),
        w1=f32(w1), b1=f32(b1).reshape(E * HC, 128),
        w2=f32(w2), b2=f32(b2),
        cAT=AT, cdec=dec, c128=c128, cid=ident,
    )
    xs = f32(x)
    return [dict(common, x=np.ascontiguousarray(xs[b])) for b in range(xs.shape[0])]


def kernel(L0_attn_x, L0_attn_W_in, L0_attn_b_in, L0_attn_W_out, L0_attn_b_out,
           L0_ffn_gate_w, L0_ffn_gate_b, L0_ffn_w1, L0_ffn_b1, L0_ffn_w2,
           L0_ffn_b2):
    nc = build_program(SEQ=4096, CHUNK=512, H=4096)
    in_maps = make_in_maps(L0_attn_x, L0_attn_W_in, L0_attn_b_in,
                           L0_attn_W_out, L0_attn_b_out, L0_ffn_gate_w,
                           L0_ffn_gate_b, L0_ffn_w1, L0_ffn_b1, L0_ffn_w2,
                           L0_ffn_b2)
    res = run_bass_kernel_spmd(nc, in_maps, core_ids=list(range(N_CORES)))
    out = np.stack([np.asarray(res.results[b]["out"]) for b in range(N_CORES)],
                   axis=0)
    return out.astype(np.float32)
